# version 25
# baseline (speedup 1.0000x reference)
import numpy as np
import ml_dtypes
from contextlib import ExitStack

import concourse.bass as bass
import concourse.mybir as mybir
from concourse.bass_utils import run_bass_kernel_spmd

F32 = mybir.dt.float32
BF16 = mybir.dt.bfloat16
I32 = mybir.dt.int32
AF = mybir.ActivationFunctionType
ALU = mybir.AluOpType
AX = mybir.AxisListType

B_, L, D, H, DH, HID = 8, 4096, 1024, 16, 64, 3072
NT = L // 128
NB = L // 512
NQ = L // 512
NC_ = L // 256
CSH = 20.0
EPS = 1e-5
ENGS = ("sync", "gpsimd", "tensor", "scalar", "vector")


class Sem:
    __slots__ = ("name", "val", "h")

    def __init__(self, name):
        self.name, self.val, self.h = name, 0, None


class Plan:
    def __init__(self):
        self.steps = {e: [] for e in ENGS}
        self.sems = []

    def sem(self, name):
        s = Sem(name)
        self.sems.append(s)
        return s

    def do(self, eng, fn, wait=(), inc=None):
        self.steps[eng].append((tuple(wait), fn, inc))
        if inc is not None:
            inc[0].val += inc[1]

    def at(self, *sems):
        return tuple((s, s.val) for s in sems)

    def emit(self, nc, block):
        for s in self.sems:
            s.h = nc._semstack.enter_context(nc.semaphore(s.name))
        for eng_name in ENGS:
            steps = self.steps[eng_name]
            if not steps:
                continue
            dec = getattr(block, eng_name)

            def body(eng, steps=steps):
                last = {}
                for waits, fn, inc in steps:
                    for s, v in waits:
                        if v > 0 and last.get(id(s), -1) < v:
                            eng.wait_ge(s.h, v)
                            last[id(s)] = v
                    ins = fn(eng)
                    if inc is not None:
                        ins.then_inc(inc[0].h, inc[1])

            dec(body)


def build_nc(debug=False):
    nc = bass.Bass("TRN2", target_bir_lowering=False)
    nc._semstack = ExitStack()

    x = nc.dram_tensor("x", [L, D], F32, kind="ExternalInput").ap()
    wqkvT = nc.dram_tensor("wqkvT", [D, 3 * D], F32, kind="ExternalInput").ap()
    wprojT = nc.dram_tensor("wprojT", [D, D], F32, kind="ExternalInput").ap()
    wgT = nc.dram_tensor("wgT", [D, HID], I32, kind="ExternalInput").ap()
    wuT = nc.dram_tensor("wuT", [D, HID], I32, kind="ExternalInput").ap()
    wdT = nc.dram_tensor("wdT", [HID, D], I32, kind="ExternalInput").ap()
    ident = nc.dram_tensor("ident", [128, 128], BF16, kind="ExternalInput").ap()
    maskbig = nc.dram_tensor("maskbig", [128, 128], BF16, kind="ExternalInput").ap()
    identf = nc.dram_tensor("identf", [65, 65], F32, kind="ExternalInput").ap()
    out = nc.dram_tensor("out", [L, D], F32, kind="ExternalOutput").ap()

    skind = "ExternalOutput" if debug else "Internal"
    qscr = nc.dram_tensor("qscr", [D, L], BF16, kind="Internal").ap()
    vscr = nc.dram_tensor("vscr", [NT, 128, 1040], BF16, kind="Internal").ap()
    wgbf = nc.dram_tensor("wgbf", [D, HID], BF16, kind="Internal").ap()
    wubf = nc.dram_tensor("wubf", [D, HID], BF16, kind="Internal").ap()
    wdbf = nc.dram_tensor("wdbf", [HID, D], BF16, kind="Internal").ap()
    x1scr = nc.dram_tensor("x1scr", [L, D], F32, kind=skind).ap()
    U16 = mybir.dt.uint16
    if debug:
        qdump = nc.dram_tensor("qdump", [D, L], U16, kind="ExternalOutput").ap()
        vdump = nc.dram_tensor("vdump", [NT, 128, 1040], U16, kind="ExternalOutput").ap()
        xndump = nc.dram_tensor("xndump", [L, D], U16, kind="ExternalOutput").ap()
        attdump = nc.dram_tensor("attdump", [NQ, 128, 8 * 512], U16, kind="ExternalOutput").ap()
        statdump = nc.dram_tensor("statdump", [6, 128, NT], F32, kind="ExternalOutput").ap()

    P = Plan()
    ld = P.sem("ld")
    sld = P.sem("sld")
    gld = P.sem("gld")
    gst = P.sem("gst")
    pe = P.sem("pe")
    ac = P.sem("ac")
    ve = P.sem("ve")

    with ExitStack() as glob:
        e = glob.enter_context
        sident = e(nc.sbuf_tensor("sident", [128, 128], BF16))
        smask = e(nc.sbuf_tensor("smask", [128, 128], BF16))
        sidentf = e(nc.sbuf_tensor("sidentf", [65, 65], F32))
        eps_sb = e(nc.sbuf_tensor("eps_sb", [128, 1], F32))
        bt_nm = e(nc.sbuf_tensor("bt_nm", [128, NT], F32))
        bt_rstd = e(nc.sbuf_tensor("bt_rstd", [128, NT], F32))
        ncsh_sb = e(nc.sbuf_tensor("ncsh_sb", [128, 1], F32))
        P.do("vector", lambda v: v.memset(eps_sb[:], EPS), inc=(ve, 1))
        P.do("vector", lambda v: v.memset(ncsh_sb[:], -CSH), inc=(ve, 1))

        P.do("sync", lambda s: s.dma_start(out=sident[:], in_=ident), inc=(ld, 16))
        P.do("sync", lambda s: s.dma_start(out=smask[:], in_=maskbig), inc=(ld, 16))
        P.do("sync", lambda s: s.dma_start(out=sidentf[:], in_=identf), inc=(ld, 16))
        consts_ld = P.at(ld)

        phab = glob.enter_context(ExitStack())
        KT = phab.enter_context(
            nc.sbuf_tensor("KT", [128, 8 * L], BF16))

        with ExitStack() as pha:
            ea = pha.enter_context
            wq_sb = ea(nc.sbuf_tensor("wq_sb", [128, 8 * 3 * D], BF16))
            xbuf = [ea(nc.sbuf_tensor(f"xbuf{i}", [128, D], F32)) for i in range(4)]
            xnbuf = [ea(nc.sbuf_tensor(f"xnbuf{i}", [128, D], BF16)) for i in range(4)]
            xnT = ea(nc.sbuf_tensor("xnT", [128, 8 * 512], BF16))
            qbuf = [ea(nc.sbuf_tensor(f"qbuf{i}", [128, 512], BF16)) for i in range(4)]
            vxbuf = [ea(nc.sbuf_tensor(f"vxbuf{i}", [128, 1040], BF16)) for i in range(2)]
            sqscr = ea(nc.sbuf_tensor("sqscr", [128, D], F32))
            st_sum = ea(nc.sbuf_tensor("st_sum", [128, NT], F32))
            st_ssq = ea(nc.sbuf_tensor("st_ssq", [128, NT], F32))
            st_nm = ea(nc.sbuf_tensor("st_nm", [128, NT], F32))
            st_m2 = ea(nc.sbuf_tensor("st_m2", [128, NT], F32))
            st_var = ea(nc.sbuf_tensor("st_var", [128, NT], F32))
            st_lnv = ea(nc.sbuf_tensor("st_lnv", [128, NT], F32))
            st_rstd = ea(nc.sbuf_tensor("st_rstd", [128, NT], F32))
            ps_tr = [ea(nc.psum_tensor(f"ps_tr{i}", [128, 1024], BF16)) for i in range(2)]
            ps_qkv = [ea(nc.psum_tensor(f"ps_qkv{i}", [128, 512], F32)) for i in range(4)]
            ps_v = ea(nc.psum_tensor("ps_v", [128, 1024], F32))

            marks = {}
            for j in range(8):
                P.do("gpsimd", lambda g, j=j: g.dma_start(
                    out=wq_sb[:, 3 * D * j:3 * D * (j + 1)],
                    in_=wqkvT[128 * j:128 * (j + 1), :]), inc=(gld, 16))
                marks[("wqld", j)] = P.at(gld)
            wq_ld = P.at(gld)
            for i in range(2):
                P.do("vector", lambda v, i=i: v.memset(vxbuf[i][:], 1.0), inc=(ve, 1))

            def plan_ln(i):
                bi = i % 4
                wt = list(marks.get(("apply", i - 4), ()))
                P.do("sync", lambda s, i=i, bi=bi: s.dma_start(
                    out=xbuf[bi][:], in_=x[128 * i:128 * (i + 1), :]),
                    wait=wt, inc=(ld, 16))
                xld = P.at(ld)
                cc = (i, i + 1)
                P.do("vector", lambda v, bi=bi, cc=cc: v.reduce_sum(
                    st_sum[:, cc[0]:cc[1]], xbuf[bi][:], AX.X), wait=xld, inc=(ve, 1))
                P.do("vector", lambda v, bi=bi, cc=cc: v.scalar_tensor_tensor(
                    sqscr[:], xbuf[bi][:], 1.0, xbuf[bi][:], ALU.mult, ALU.mult,
                    accum_out=st_ssq[:, cc[0]:cc[1]]), inc=(ve, 1))
                P.do("vector", lambda v, cc=cc: v.tensor_scalar_mul(
                    st_nm[:, cc[0]:cc[1]], st_sum[:, cc[0]:cc[1]], -1.0 / D),
                    wait=P.at(ve), inc=(ve, 1))
                P.do("vector", lambda v, cc=cc: v.tensor_mul(
                    st_m2[:, cc[0]:cc[1]], st_nm[:, cc[0]:cc[1]], st_nm[:, cc[0]:cc[1]]),
                    wait=P.at(ve), inc=(ve, 1))
                P.do("vector", lambda v, cc=cc: v.scalar_tensor_tensor(
                    st_var[:, cc[0]:cc[1]], st_ssq[:, cc[0]:cc[1]], 1.0 / D,
                    st_m2[:, cc[0]:cc[1]], ALU.mult, ALU.subtract),
                    wait=P.at(ve), inc=(ve, 1))
                stats = P.at(ve)
                P.do("scalar", lambda s, cc=cc: s.activation(
                    st_lnv[:, cc[0]:cc[1]], st_var[:, cc[0]:cc[1]], AF.Ln, bias=eps_sb[:]),
                    wait=stats, inc=(ac, 1))
                P.do("scalar", lambda s, cc=cc: s.activation(
                    st_rstd[:, cc[0]:cc[1]], st_lnv[:, cc[0]:cc[1]], AF.Exp, scale=-0.5),
                    wait=P.at(ac), inc=(ac, 1))
                marks[("rstd", i)] = P.at(ac)

            def plan_apply(i):
                bi = i % 4
                cc = (i, i + 1)
                P.do("vector", lambda v, bi=bi, cc=cc: v.tensor_scalar(
                    xnbuf[bi][:], xbuf[bi][:], st_nm[:, cc[0]:cc[1]],
                    st_rstd[:, cc[0]:cc[1]], ALU.add, ALU.mult),
                    wait=marks[("rstd", i)] + marks.get(("trans", i - 4), ()) +
                    (marks.get(("xndump", i - 4), ()) if debug else ()), inc=(ve, 1))
                marks[("apply", i)] = P.at(ve)
                if debug:
                    P.do("sync", lambda s, i=i, bi=bi: s.dma_start(
                        out=xndump[128 * i:128 * (i + 1), :].bitcast(U16),
                        in_=xnbuf[bi][:].bitcast(U16)),
                        wait=marks[("apply", i)], inc=(ld, 16))
                    marks[("xndump", i)] = P.at(ld)

            def plan_trans(i):
                t = i % 4
                bi = i % 4
                for j in range(8):
                    P.do("tensor", lambda t_, i=i, bi=bi, j=j: t_.matmul(
                        ps_tr[i % 2][:, 128 * j:128 * (j + 1)],
                        xnbuf[bi][:, 128 * j:128 * (j + 1)], sident[:],
                        is_transpose=True, start=True, stop=True,
                        skip_group_check=True),
                        wait=(marks[("apply", i)] + consts_ld +
                              marks.get(("trevac", i - 2), ())) if j == 0 else (),
                        inc=(pe, 1) if j == 7 else None)
                marks[("trans", i)] = P.at(pe)
                P.do("vector", lambda v, i=i, t=t: v.tensor_copy(
                    xnT[:].rearrange("p (j c) -> p j c", c=512)[:, :, 128 * t:128 * (t + 1)],
                    ps_tr[i % 2][:].rearrange("p (j c) -> p j c", c=128)),
                    wait=marks[("trans", i)], inc=(ve, 1))
                marks[("trevac", i)] = P.at(ve)

            for t_ in range(4):
                plan_ln(t_)
            for t_ in range(4):
                plan_apply(t_)
            for b in range(NB):
                i = 4 * b + 3
                for t_ in range(4):
                    plan_trans(4 * b + t_)
                if True:
                    xnT_ready = marks[("trevac", i)]
                    for w_ in range(16):
                        wg_ = 16 * b + w_
                        wbi = w_ % 4
                        for j in range(8):
                            P.do("tensor", lambda t_, wbi=wbi, j=j, w_=w_: t_.matmul(
                                ps_qkv[wbi][:],
                                wq_sb[:, 3 * D * j + 128 * w_:3 * D * j + 128 * (w_ + 1)],
                                xnT[:, 512 * j:512 * (j + 1)],
                                start=(j == 0), stop=(j == 7)),
                                wait=((xnT_ready +
                                      marks.get(("qkevac", wg_ - 4), ())) if j == 0
                                      else ()) + marks[("wqld", j)],
                                inc=(pe, 1) if j == 7 else None)
                        mm = P.at(pe)
                        if w_ < 8:
                            qc = 8 * b + w_
                            P.do("scalar", lambda s, wbi=wbi, qc=qc: s.copy(
                                qbuf[qc % 4][:], ps_qkv[wbi][:]),
                                wait=mm + marks.get(("qspill", qc - 4), ()), inc=(ac, 1))
                            marks[("qkevac", wg_)] = P.at(ac)
                            P.do("gpsimd", lambda g, w_=w_, b=b, qc=qc: g.dma_start(
                                out=qscr[128 * w_:128 * (w_ + 1), 512 * b:512 * (b + 1)],
                                in_=qbuf[qc % 4][:]),
                                wait=marks[("qkevac", wg_)], inc=(gst, 16))
                            marks[("qspill", qc)] = P.at(gst)
                            if w_ == 7 and b + 1 < NB:
                                for t_ in range(4):
                                    plan_ln(4 * (b + 1) + t_)
                        else:
                            kk = w_ - 8
                            P.do("scalar", lambda s, kk=kk, b=b, wbi=wbi: s.copy(
                                KT[:, L * kk + 512 * b:L * kk + 512 * (b + 1)],
                                ps_qkv[wbi][:]), wait=mm, inc=(ac, 1))
                            marks[("qkevac", wg_)] = P.at(ac)
                    if b + 1 < NB:
                        for t_ in range(4):
                            plan_apply(4 * (b + 1) + t_)
                    for tp in range(4):
                        tg = 4 * b + tp
                        vbi = tg % 2
                        for hf in range(2):
                            for j in range(8):
                                P.do("tensor", lambda t_, tp=tp, hf=hf, j=j: t_.matmul(
                                    ps_v[:, 512 * hf:512 * (hf + 1)],
                                    xnT[:, 512 * j + 128 * tp:512 * j + 128 * (tp + 1)],
                                    wq_sb[:, 3 * D * j + 2 * D + 512 * hf:
                                          3 * D * j + 2 * D + 512 * (hf + 1)],
                                    start=(j == 0), stop=(j == 7)),
                                    wait=marks.get(("vevac", tg - 1), ())
                                    if (j == 0 and hf == 0) else (),
                                    inc=(pe, 1) if (j == 7 and hf == 1) else None)
                        vm = P.at(pe)
                        P.do("vector", lambda v, vbi=vbi: v.tensor_copy(
                            vxbuf[vbi][:].rearrange("p (h c) -> p h c", c=65)[:, :, 0:64],
                            ps_v[:].rearrange("p (h c) -> p h c", c=64)),
                            wait=vm + marks.get(("vspill", tg - 2), ()), inc=(ve, 1))
                        marks[("vevac", tg)] = P.at(ve)
                        P.do("gpsimd", lambda g, b=b, tp=tp, vbi=vbi: g.dma_start(
                            out=vscr[4 * b + tp], in_=vxbuf[vbi][:]),
                            wait=marks[("vevac", tg)], inc=(gst, 16))
                        marks[("vspill", tg)] = P.at(gst)

            if debug:
                for n_, t_ in ((0, st_sum), (1, st_ssq), (2, st_nm), (3, st_rstd),
                               (4, st_m2), (5, st_var)):
                    P.do("sync", lambda s, n_=n_, t_=t_: s.dma_start(
                        out=statdump[n_], in_=t_[:]),
                        wait=P.at(ve, ac), inc=(ld, 16))
            barA = P.at(ld, gld, gst, pe, ac, ve)

        with ExitStack() as phb:
            eb = phb.enter_context
            AED = 7
            qt_sb = [eb(nc.sbuf_tensor(f"qt_sb{i}", [128, 8 * 512], BF16))
                     for i in range(2)]
            V_sb = eb(nc.sbuf_tensor("V_sb", [128, NT * 1040], BF16))
            ae = [eb(nc.sbuf_tensor(f"ae{i}", [128, 1024], BF16))
                  for i in range(AED)]
            attok = [eb(nc.sbuf_tensor(f"attok{i}", [128, 512], BF16))
                     for i in range(2)]
            attoutT = eb(nc.sbuf_tensor("attoutT", [128, 8 * 512], BF16))
            rec_sb = eb(nc.sbuf_tensor("rec_sb", [128, 8], F32))
            xtile = [eb(nc.sbuf_tensor(f"xtile{i}", [128, D], F32)) for i in range(2)]
            xp_sb = [eb(nc.sbuf_tensor(f"xp_sb{i}", [128, D], F32)) for i in range(2)]
            sqscr2 = eb(nc.sbuf_tensor("sqscr2", [128, D], BF16))
            bt_sum = eb(nc.sbuf_tensor("bt_sum", [128, NT], F32))
            bt_ssq = eb(nc.sbuf_tensor("bt_ssq", [128, NT], F32))
            bt_m2 = eb(nc.sbuf_tensor("bt_m2", [128, NT], F32))
            bt_var = eb(nc.sbuf_tensor("bt_var", [128, NT], F32))
            bt_lnv = eb(nc.sbuf_tensor("bt_lnv", [128, NT], F32))
            wp_sb = eb(nc.sbuf_tensor("wp_sb", [128, 8 * D], BF16))

            pool = eb(nc.psum_tensor("pbpool", [128, 4096], F32))
            poolb = pool.bitcast(BF16)
            psc = [pool[:, 0:1024], pool[:, 1024:2048]]
            trscratch = [poolb[:, 4096:4608], poolb[:, 6144:6656]]

            def av_region(p, hh, s):
                b = 4 + 2 * p + hh
                return pool[:, 512 * b + 65 * s:512 * b + 65 * (s + 1)]

            for j in range(8):
                P.do("gpsimd", lambda g, j=j: g.dma_start(
                    out=wp_sb[:, D * j:D * (j + 1)],
                    in_=wprojT[128 * j:128 * (j + 1), :]),
                    wait=barA if j == 0 else (), inc=(gld, 16))
            wp_ld = P.at(gld)
            precast = ()

            def plan_qload(q):
                P.do("sync", lambda s, q=q: s.dma_start(
                    out=qt_sb[q % 2][:].rearrange("p (j c) -> p j c", c=512),
                    in_=qscr[:, 512 * q:512 * (q + 1)].rearrange(
                        "(j p) c -> p j c", p=128)),
                    wait=(barA if q <= 1 else ()) +
                    marks.get(("scores_done", q - 2), ()), inc=(ld, 16))

            plan_qload(0)
            pending_stages = []
            steps_glob = []
            aidx = {}
            PSTEPS = 7

            def plan_sc_q(qt_, hp, kb, qv):
                d = kb - 4 * qt_
                off = 128 * d if d > 0 else 0
                wt = marks.get(("exp", qt_, hp, kb - 2), ())
                if kb <= 1:
                    if hp > 0:
                        wt = wt + marks.get(
                            ("exp", qt_, hp - 1, 4 * qt_ + 2 + kb), ())
                    else:
                        wt = wt + marks.get(
                            ("exp", qt_ - 1, 7, 4 * (qt_ - 1) + 2 + kb), ())
                if kb == 0:
                    wt = wt + qv + wp_ld
                    if hp == 0 and qt_ == 0:
                        wt = wt + barA
                for (hh, p0, tp) in ((0, 0, (0, 0)), (1, 64, (64, 0))):
                    P.do("tensor", lambda t_, kb=kb, hh=hh, p0=p0, tp=tp,
                         off=off, hp=hp, qt_=qt_: t_.matmul(
                             psc[kb % 2][:, 512 * hh + off:512 * (hh + 1)],
                             KT[p0:p0 + 64,
                                L * hp + 128 * kb:L * hp + 128 * (kb + 1)],
                             qt_sb[qt_ % 2][p0:p0 + 64,
                                            512 * hp + off:512 * (hp + 1)],
                             start=True, stop=True, tile_position=tp,
                             skip_group_check=True),
                         wait=wt if hh == 0 else (),
                         inc=(pe, 1) if hh == 1 else None)
                marks[("sc", qt_, hp, kb)] = P.at(pe)

            def emit_step(qt_, hp, kb, qv):
                if ("exp", qt_, hp, kb) in marks:
                    return
                if ("sc", qt_, hp, kb) not in marks:
                    plan_sc_q(qt_, hp, kb, qv)
                d = kb - 4 * qt_
                step = len(steps_glob)
                steps_glob.append((qt_, hp, kb))
                ai = step % AED
                aidx[(qt_, hp, kb)] = ai
                aefree = (marks.get(("av",) + steps_glob[step - AED], ())
                          if step >= AED else ())
                wex = marks[("sc", qt_, hp, kb)] + aefree
                if d <= 0:
                    P.do("scalar", lambda s, kb=kb, ai=ai: s.activation(
                        ae[ai][:], psc[kb % 2][:], AF.Exp,
                        bias=ncsh_sb[:], scale=0.125),
                        wait=wex, inc=(ac, 1))
                else:
                    for hh in range(2):
                        P.do("scalar", lambda s, kb=kb, ai=ai, hh=hh,
                             d=d: s.activation(
                                 ae[ai][:, 512 * hh + 128 * d:
                                        512 * (hh + 1)],
                                 psc[kb % 2][:, 512 * hh + 128 * d:
                                             512 * (hh + 1)],
                                 AF.Exp, bias=ncsh_sb[:], scale=0.125),
                             wait=wex if hh == 0 else (), inc=(ac, 1))
                marks[("exp", qt_, hp, kb)] = P.at(ac)
                if d >= 0:
                    for hh in range(2):
                        P.do("vector", lambda v, ai=ai, hh=hh, d=d:
                             v.tensor_mul(
                                 ae[ai][:, 512 * hh + 128 * d:
                                        512 * hh + 128 * (d + 1)],
                                 ae[ai][:, 512 * hh + 128 * d:
                                        512 * hh + 128 * (d + 1)],
                                 smask[:, 0:128]),
                             wait=(marks[("exp", qt_, hp, kb)] + consts_ld)
                             if hh == 0 else (),
                             inc=(ve, 1))
                    marks[("mask", qt_, hp, kb)] = P.at(ve)

            for qt in range(NQ):
                if qt == 3:
                    for j in range(8):
                        P.do("gpsimd", lambda g, j=j: g.dma_start(
                            out=wgbf[128 * j:128 * (j + 1), :],
                            in_=wgT[128 * j:128 * (j + 1), :]), inc=(gst, 16))
                    for j in range(8):
                        P.do("gpsimd", lambda g, j=j: g.dma_start(
                            out=wubf[128 * j:128 * (j + 1), :],
                            in_=wuT[128 * j:128 * (j + 1), :]), inc=(gst, 16))
                    for k in range(24):
                        P.do("gpsimd", lambda g, k=k: g.dma_start(
                            out=wdbf[128 * k:128 * (k + 1), :],
                            in_=wdT[128 * k:128 * (k + 1), :]), inc=(gst, 16))
                    precast = P.at(gst)
                if qt + 1 < NQ:
                    plan_qload(qt + 1)
                for kb in range(4 * qt, 4 * qt + 4):
                    P.do("sync", lambda s, kb=kb: s.dma_start(
                        out=V_sb[:, 1040 * kb:1040 * (kb + 1)], in_=vscr[kb]),
                        wait=barA if qt == 0 and kb == 0 else (), inc=(ld, 16))
                qv_ld = P.at(ld)

                nkb = 4 * qt + 4
                for hp in range(8):
                    hpg = 8 * qt + hp
                    p = hp % 2
                    for kb in range(nkb):
                        emit_step(qt, hp, kb, qv_ld)
                        if (kb + 1 < nkb and
                                ("sc", qt, hp, kb + 1) not in marks):
                            plan_sc_q(qt, hp, kb + 1, qv_ld)
                        d = kb - 4 * qt
                        ai = aidx[(qt, hp, kb)]
                        wav = (marks[("exp", qt, hp, kb)] +
                               marks.get(("mask", qt, hp, kb), ()))
                        if kb == 0:
                            wav = wav + marks.get(("cpy", hpg - 2), ())
                            if hp == 0:
                                wav = wav + marks.get(("xpadd", qt - 1, 0), ())
                            elif hp == 1:
                                wav = wav + marks.get(("xpadd", qt - 1, 3), ())
                        mml = [(s, hh) for s in range(4) if d <= s
                               for hh in range(2)]
                        for i_, (s, hh) in enumerate(mml):
                            P.do("tensor", lambda t_, kb=kb, s=s, hh=hh, hp=hp,
                                 p=p, qt=qt, ai=ai: t_.matmul(
                                     av_region(p, hh, s),
                                     ae[ai][:, 512 * hh + 128 * s:
                                            512 * hh + 128 * (s + 1)],
                                     V_sb[:, 1040 * kb + 65 * (2 * hp + hh):
                                          1040 * kb + 65 * (2 * hp + hh) + 65],
                                     start=(kb == 0 and s == 0),
                                     stop=(kb == 4 * qt + s),
                                     skip_group_check=True),
                                 wait=wav if i_ == 0 else (),
                                 inc=(pe, 1) if i_ == len(mml) - 1 else None)
                        marks[("av", qt, hp, kb)] = P.at(pe)
                        if pending_stages:
                            pending_stages.pop(0)()
                        if hp + 1 < 8 and kb >= nkb - 2:
                            emit_step(qt, hp + 1, kb - (nkb - 2), qv_ld)
                    if hp == 7:
                        marks[("scores_done", qt)] = P.at(pe)

                    def make_evac(qt=qt, hp=hp, hpg=hpg, p=p,
                                  avm=marks[("av", qt, hp, nkb - 1)]):
                        def s_recip():
                            for hh in range(2):
                                b = 4 + 2 * p + hh
                                P.do("vector", lambda v, b=b, hh=hh: v.reciprocal(
                                    rec_sb[:, 4 * hh:4 * hh + 4],
                                    pool[:, 512 * b:512 * b + 260].rearrange(
                                        "p (s w) -> p s w", w=65)[:, :, 64]),
                                    wait=avm if hh == 0 else (), inc=(ve, 1))
                            marks[("recip", hpg)] = P.at(ve)

                        def s_norm():
                            wt = (marks[("recip", hpg)] +
                                  marks.get(("fwdT", hpg - 2), ()))
                            i_ = 0
                            for s in range(4):
                                for hh in range(2):
                                    P.do("vector", lambda v, s=s, hh=hh, p=p:
                                         v.tensor_scalar_mul(
                                             attok[p][:, 128 * s + 64 * hh:
                                                      128 * s + 64 * (hh + 1)],
                                             av_region(p, hh, s)[:, 0:64],
                                             rec_sb[:, 4 * hh + s:4 * hh + s + 1]),
                                         wait=wt if i_ == 0 else (), inc=(ve, 1))
                                    i_ += 1
                            marks[("norm", hpg)] = P.at(ve)

                        def s_fwdT():
                            wt = (marks[("norm", hpg)] +
                                  marks.get(("cpy", hpg - 2), ()))
                            for s in range(4):
                                P.do("tensor", lambda t_, s=s, p=p: t_.matmul(
                                    trscratch[p][:, 128 * s:128 * (s + 1)],
                                    attok[p][:, 128 * s:128 * (s + 1)], sident[:],
                                    is_transpose=True, start=True, stop=True,
                                    skip_group_check=True),
                                    wait=wt if s == 0 else (),
                                    inc=(pe, 1) if s == 3 else None)
                            marks[("fwdT", hpg)] = P.at(pe)

                        def s_cpy():
                            wt = marks[("fwdT", hpg)] + \
                                (marks.get(("attdump", qt - 1), ())
                                 if debug else ())
                            P.do("vector", lambda v, hp=hp, p=p: v.tensor_copy(
                                attoutT[:, 512 * hp:512 * (hp + 1)],
                                trscratch[p][:]),
                                wait=wt, inc=(ve, 1))
                            marks[("cpy", hpg)] = P.at(ve)

                        return [s_recip, s_norm, s_fwdT, s_cpy]

                    while pending_stages:
                        pending_stages.pop(0)()
                    pending_stages.extend(make_evac())
                if qt + 1 < NQ:
                    qvp = P.at(ld)
                    for kb in range(PSTEPS):
                        emit_step(qt + 1, 0, kb, qvp)
                while pending_stages:
                    pending_stages.pop(0)()
                attout_rdy = marks[("cpy", 8 * qt + 7)]
                if debug:
                    P.do("sync", lambda s, qt=qt: s.dma_start(
                        out=attdump[qt].bitcast(U16), in_=attoutT[:].bitcast(U16)),
                        wait=attout_rdy, inc=(ld, 16))
                    marks[("attdump", qt)] = P.at(ld)

                def plan_xload(tp, qt=qt):
                    wt = (marks.get(("xpadd", qt, tp - 2), ()) if tp >= 2
                          else marks.get(("xpadd", qt - 1, tp + 2), ()))
                    P.do("sync", lambda s, qt=qt, tp=tp: s.dma_start(
                        out=xtile[tp % 2][:],
                        in_=x[512 * qt + 128 * tp:512 * qt + 128 * (tp + 1), :]),
                        wait=wt, inc=(ld, 16))
                    marks[("xld", qt, tp)] = P.at(ld)

                for tp in range(2):
                    plan_xload(tp)

                for s_ in range(4):
                    pcol = 2048 if s_ == 0 else 3072
                    for hf in range(2):
                        for j in range(8):
                            wt_ = ()
                            if j == 0 and hf == 0:
                                wt_ = (attout_rdy if s_ < 2 else
                                       marks[("xpadd", qt, s_ - 1)])
                            P.do("tensor", lambda t_, s_=s_, hf=hf, j=j,
                                 pcol=pcol: t_.matmul(
                                     pool[:, pcol + 512 * hf:
                                          pcol + 512 * (hf + 1)],
                                     attoutT[:, 512 * j + 128 * s_:
                                             512 * j + 128 * (s_ + 1)],
                                     wp_sb[:, D * j + 512 * hf:
                                           D * j + 512 * (hf + 1)],
                                     start=(j == 0), stop=(j == 7),
                                     skip_group_check=True),
                                 wait=wt_,
                                 inc=(pe, 1) if (j == 7 and hf == 1) else None)
                    marks[("projm", qt, s_)] = P.at(pe)
                    P.do("vector", lambda v, s_=s_, pcol=pcol: v.tensor_add(
                        xp_sb[s_ % 2][:], xtile[s_ % 2][:],
                        pool[:, pcol:pcol + 1024]),
                        wait=marks[("projm", qt, s_)] + marks[("xld", qt, s_)] +
                        (marks.get(("xpspill", qt, s_ - 2), ()) if s_ >= 2
                         else marks.get(("xpspill", qt - 1, s_ + 2), ())),
                        inc=(ve, 1))
                    marks[("xpadd", qt, s_)] = P.at(ve)
                    if s_ + 2 < 4:
                        plan_xload(s_ + 2)
                    P.do("gpsimd", lambda g, qt=qt, s_=s_: g.dma_start(
                        out=x1scr[512 * qt + 128 * s_:512 * qt + 128 * (s_ + 1), :],
                        in_=xp_sb[s_ % 2][:]),
                        wait=marks[("xpadd", qt, s_)], inc=(gst, 16))
                    marks[("xpspill", qt, s_)] = P.at(gst)
                    cc = (4 * qt + s_, 4 * qt + s_ + 1)
                    P.do("vector", lambda v, cc=cc, s_=s_: v.reduce_sum(
                        bt_sum[:, cc[0]:cc[1]], xp_sb[s_ % 2][:], AX.X),
                        inc=(ve, 1))
                    P.do("vector", lambda v, cc=cc, s_=s_: v.scalar_tensor_tensor(
                        sqscr2[:], xp_sb[s_ % 2][:], 1.0, xp_sb[s_ % 2][:],
                        ALU.mult, ALU.mult,
                        accum_out=bt_ssq[:, cc[0]:cc[1]]), inc=(ve, 1))
                    P.do("vector", lambda v, cc=cc: v.tensor_scalar_mul(
                        bt_nm[:, cc[0]:cc[1]], bt_sum[:, cc[0]:cc[1]], -1.0 / D),
                        wait=P.at(ve), inc=(ve, 1))
                    P.do("vector", lambda v, cc=cc: v.tensor_mul(
                        bt_m2[:, cc[0]:cc[1]], bt_nm[:, cc[0]:cc[1]],
                        bt_nm[:, cc[0]:cc[1]]), wait=P.at(ve), inc=(ve, 1))
                    P.do("vector", lambda v, cc=cc: v.scalar_tensor_tensor(
                        bt_var[:, cc[0]:cc[1]], bt_ssq[:, cc[0]:cc[1]], 1.0 / D,
                        bt_m2[:, cc[0]:cc[1]], ALU.mult, ALU.subtract),
                        wait=P.at(ve), inc=(ve, 1))
                    marks[("bstat", qt, s_)] = P.at(ve)

                def make_ln2fin(qt=qt):
                    outl = []
                    for s_ in range(4):
                        def fin(s_=s_, qt=qt):
                            cc = (4 * qt + s_, 4 * qt + s_ + 1)
                            P.do("scalar", lambda s, cc=cc: s.activation(
                                bt_lnv[:, cc[0]:cc[1]], bt_var[:, cc[0]:cc[1]],
                                AF.Ln, bias=eps_sb[:]),
                                wait=marks[("bstat", qt, s_)], inc=(ac, 1))
                            P.do("scalar", lambda s, cc=cc: s.activation(
                                bt_rstd[:, cc[0]:cc[1]], bt_lnv[:, cc[0]:cc[1]],
                                AF.Exp, scale=-0.5), wait=P.at(ac), inc=(ac, 1))
                            marks[("bt_rstd", qt, s_)] = P.at(ac)
                        outl.append(fin)
                    return outl

                pending_stages.extend([lambda: None] * min(12, 4 * (qt + 1)))
                pending_stages.extend(make_ln2fin())
                if qt == NQ - 1:
                    while pending_stages:
                        pending_stages.pop(0)()

            barB = P.at(ld, gld, gst, pe, ac, ve)

        phab.close()

        with ExitStack() as phc:
            ec = phc.enter_context
            wg_sb = ec(nc.sbuf_tensor("wg_sb", [128, 8 * HID], BF16))
            xpblk = [ec(nc.sbuf_tensor(f"xpblk{i}", [128, 2 * D], F32))
                     for i in range(3)]
            wu_sb = ec(nc.sbuf_tensor("wu_sb", [128, 8 * HID], BF16))
            wd_sb = ec(nc.sbuf_tensor("wd_sb", [128, 24 * D], BF16))
            xn2blk = [ec(nc.sbuf_tensor(f"xn2blk{i}", [128, 2 * D], BF16)) for i in range(2)]
            xn2T = ec(nc.sbuf_tensor("xn2T", [128, 8 * 256], BF16))
            hT = ec(nc.sbuf_tensor("hT", [128, 24 * 256], BF16))
            silu_sb = [ec(nc.sbuf_tensor(f"silu{i}", [128, 256], F32)) for i in range(2)]
            out_sb = [ec(nc.sbuf_tensor(f"out_sb{i}", [128, D], F32)) for i in range(2)]
            ps_tr2 = [ec(nc.psum_tensor(f"ps_tr2{i}", [128, 1024], BF16)) for i in range(2)]
            ps_gu = [ec(nc.psum_tensor(f"ps_gu{i}", [128, 512], F32)) for i in range(2)]
            ps_d = [ec(nc.psum_tensor(f"ps_d{i}", [128, 1024], F32)) for i in range(2)]

            wrdy = marks[("scores_done", NQ - 1)] + precast
            for j in range(8):
                P.do("scalar", lambda s, j=j: s.dma_start(
                    out=wg_sb[:, HID * j:HID * (j + 1)],
                    in_=wgbf[128 * j:128 * (j + 1), :]),
                    wait=wrdy if j == 0 else (), inc=(sld, 16))
                marks[("wg_ld", j)] = P.at(sld)
            for k in range(24):
                P.do("gpsimd", lambda g, k=k: g.dma_start(
                    out=wd_sb[:, D * k:D * (k + 1)],
                    in_=wdbf[128 * k:128 * (k + 1), :]),
                    wait=(barB + precast) if k == 0 else (), inc=(gld, 16))
                marks[("wd_ld", k)] = P.at(gld)

            def plan_cdma(c):
                P.do("sync", lambda s, c=c: s.dma_start(
                    out=xpblk[c % 3][:].rearrange("p (t d) -> p t d", d=D),
                    in_=x1scr[256 * c:256 * (c + 1), :].rearrange(
                        "(t p) d -> p t d", p=128)),
                    wait=wrdy + marks.get(("outev", c - 3, 1), ()), inc=(ld, 16))
                marks[("cld", c)] = P.at(ld)

            def plan_xn2(c):
                for t_ in range(2):
                    P.do("vector", lambda v, c=c, t_=t_: v.tensor_scalar(
                        xn2blk[c % 2][:, D * t_:D * (t_ + 1)],
                        xpblk[c % 3][:, D * t_:D * (t_ + 1)],
                        bt_nm[:, 2 * c + t_:2 * c + t_ + 1],
                        bt_rstd[:, 2 * c + t_:2 * c + t_ + 1],
                        ALU.add, ALU.mult),
                        wait=(marks[("cld", c)] + barB +
                              marks.get(("ctr", c - 2), ()))
                        if t_ == 0 else (), inc=(ve, 1))
                marks[("xn2c", c)] = P.at(ve)

            plan_cdma(0)
            for j in range(2):
                P.do("sync", lambda s, j=j: s.dma_start(
                    out=wu_sb[:, HID * j:HID * (j + 1)],
                    in_=wubf[128 * j:128 * (j + 1), :]),
                    wait=wrdy if j == 0 else (), inc=(ld, 16))
                marks[("wu_ld", j)] = P.at(ld)
            plan_cdma(1)
            for j in range(2, 8):
                P.do("sync", lambda s, j=j: s.dma_start(
                    out=wu_sb[:, HID * j:HID * (j + 1)],
                    in_=wubf[128 * j:128 * (j + 1), :]), inc=(ld, 16))
                marks[("wu_ld", j)] = P.at(ld)
            plan_xn2(0)
            plan_xn2(1)

            for c in range(NC_):
                cbi = c % 2
                cxi = c % 3
                if c + 2 < NC_:
                    plan_cdma(c + 2)
                cld = marks[("cld", c)]
                for tpp in range(2):
                    for j in range(8):
                        P.do("tensor", lambda t_, tpp=tpp, j=j, cbi=cbi: t_.matmul(
                            ps_tr2[tpp][:, 128 * j:128 * (j + 1)],
                            xn2blk[cbi][:, D * tpp + 128 * j:D * tpp + 128 * (j + 1)],
                            sident[:],
                            is_transpose=True, start=True, stop=True,
                            skip_group_check=True),
                            wait=(marks[("xn2c", c)] +
                                  marks.get(("gu_done", c - 1), ()) +
                                  marks.get(("ctrev", c - 1), ()) +
                                  (barB if c == 0 else ())) if (j == 0 and tpp == 0)
                            else (),
                            inc=(pe, 1) if j == 7 else None)
                marks[("ctr", c)] = P.at(pe)
                for tpp in range(2):
                    P.do("vector", lambda v, tpp=tpp: v.tensor_copy(
                        xn2T[:].rearrange("p (j d) -> p j d", d=256)[:, :, 128 * tpp:
                                                                    128 * (tpp + 1)],
                        ps_tr2[tpp][:].rearrange("p (j d) -> p j d", d=128)),
                        wait=marks[("ctr", c)], inc=(ve, 1))
                marks[("ctrev", c)] = P.at(ve)
                for w_ in range(24):
                    wbi = w_ % 2
                    for (col0, wsb) in ((0, wg_sb), (256, wu_sb)):
                        for j in range(8):
                            wt_ = ()
                            if j == 0 and col0 == 0:
                                wt_ = (marks[("ctrev", c)] +
                                       marks.get(("hev", c, w_ - 2), ()))
                            if c == 0:
                                wt_ = wt_ + marks[("wg_ld" if col0 == 0
                                                   else "wu_ld", j)]
                            P.do("tensor", lambda t_, wbi=wbi, col0=col0, wsb=wsb,
                                 j=j, w_=w_: t_.matmul(
                                     ps_gu[wbi][:, col0:col0 + 256],
                                     wsb[:, HID * j + 128 * w_:HID * j + 128 * (w_ + 1)],
                                     xn2T[:, 256 * j:256 * (j + 1)],
                                     start=(j == 0), stop=(j == 7), skip_group_check=True),
                                 wait=wt_,
                                 inc=(pe, 1) if (j == 7 and col0 == 256) else None)
                    gum = P.at(pe)
                    P.do("scalar", lambda s, wbi=wbi: s.activation(
                        silu_sb[wbi][:], ps_gu[wbi][:, 0:256], AF.Silu, scale=1.0 / 127.0),
                        wait=gum + marks.get(("hev", c, w_ - 2), ()), inc=(ac, 1))
                    sm = P.at(ac)
                    P.do("vector", lambda v, wbi=wbi, w_=w_: v.scalar_tensor_tensor(
                        hT[:, 256 * w_:256 * (w_ + 1)], ps_gu[wbi][:, 256:512],
                        1.0 / 16129.0, silu_sb[wbi][:], ALU.mult, ALU.mult),
                        wait=sm, inc=(ve, 1))
                    marks[("hev", c, w_)] = P.at(ve)
                marks[("gu_done", c)] = P.at(pe)
                h_rdy = marks[("hev", c, 23)]
                for tpp in range(2):
                    dbi = tpp % 2
                    for hf in range(2):
                        for k in range(24):
                            wt_ = ()
                            if k == 0 and hf == 0:
                                wt_ = h_rdy + marks.get(("outev", c - 1, tpp), ())
                            if c == 0 and tpp == 0 and hf == 0:
                                wt_ = wt_ + marks[("wd_ld", k)]
                            P.do("tensor", lambda t_, dbi=dbi, hf=hf, k=k, tpp=tpp:
                                 t_.matmul(
                                     ps_d[dbi][:, 512 * hf:512 * (hf + 1)],
                                     hT[:, 256 * k + 128 * tpp:256 * k + 128 * (tpp + 1)],
                                     wd_sb[:, D * k + 512 * hf:D * k + 512 * (hf + 1)],
                                     start=(k == 0), stop=(k == 23),
                                     skip_group_check=True),
                                 wait=wt_,
                                 inc=(pe, 1) if (k == 23 and hf == 1) else None)
                    dm = P.at(pe)
                    P.do("vector", lambda v, dbi=dbi, tpp=tpp, cxi=cxi: v.tensor_add(
                        out_sb[dbi][:], xpblk[cxi][:, D * tpp:D * (tpp + 1)],
                        ps_d[dbi][:]),
                        wait=dm + cld + marks.get(("outsp", c - 1, tpp), ()), inc=(ve, 1))
                    marks[("outev", c, tpp)] = P.at(ve)
                    P.do("gpsimd", lambda g, c=c, tpp=tpp, dbi=dbi: g.dma_start(
                        out=out[256 * c + 128 * tpp:256 * c + 128 * (tpp + 1), :],
                        in_=out_sb[dbi][:]),
                        wait=marks[("outev", c, tpp)], inc=(gst, 16))
                    marks[("outsp", c, tpp)] = P.at(gst)
                if c + 2 < NC_:
                    plan_xn2(c + 2)

            if debug:
                P.do("sync", lambda s: s.dma_start(out=qdump, in_=qscr.bitcast(U16)),
                     wait=P.at(gst, ve, ac, pe), inc=(ld, 16))
                P.do("sync", lambda s: s.dma_start(out=vdump, in_=vscr.bitcast(U16)),
                     inc=(ld, 16))
                P.do("sync", lambda s: s.nop(), wait=P.at(ld))
            P.do("gpsimd", lambda g: g.nop(), wait=P.at(gst))

        block = glob.enter_context(nc.Block())
        P.emit(nc, block)

    nc._semstack.close()
    return nc


_NC_CACHE = {}


def _get_nc(debug=False):
    if debug not in _NC_CACHE:
        _NC_CACHE[debug] = build_nc(debug)
    return _NC_CACHE[debug]


def _consts():
    ident = np.eye(128, dtype=ml_dtypes.bfloat16)
    mb = (np.arange(128)[None, :] >= np.arange(128)[:, None])
    maskbig = mb.astype(ml_dtypes.bfloat16)
    identf = np.eye(65, dtype=np.float32)
    return ident, maskbig, identf


def kernel(L0_attn_x, L0_attn_w_qkv, L0_attn_w_proj, L0_ffn_w_gate,
           L0_ffn_w_up, L0_ffn_w_down, _debug=False):
    x = np.ascontiguousarray(np.asarray(L0_attn_x, dtype=np.float32))
    wqkvT = np.ascontiguousarray(np.asarray(L0_attn_w_qkv, dtype=np.float32).T)
    wprojT = np.ascontiguousarray(np.asarray(L0_attn_w_proj, dtype=np.float32).T)
    wgT = np.ascontiguousarray(np.asarray(L0_ffn_w_gate, dtype=np.int32).T)
    wuT = np.ascontiguousarray(np.asarray(L0_ffn_w_up, dtype=np.int32).T)
    wdT = np.ascontiguousarray(np.asarray(L0_ffn_w_down, dtype=np.int32).T)
    ident, maskbig, identf = _consts()

    nc = _get_nc(_debug)
    common = {"wqkvT": wqkvT, "wprojT": wprojT, "wgT": wgT, "wuT": wuT,
              "wdT": wdT, "ident": ident, "maskbig": maskbig, "identf": identf}
    in_maps = [dict(common, x=np.ascontiguousarray(x[c])) for c in range(B_)]
    res = run_bass_kernel_spmd(nc, in_maps, core_ids=list(range(B_)))
    outs = np.stack([np.asarray(res.results[c]["out"]) for c in range(B_)], axis=0)
    if _debug:
        return outs, res.results
    return outs


# version 26 (final)
# speedup vs baseline: 1.1001x; 1.1001x over previous
import numpy as np
import ml_dtypes
from contextlib import ExitStack

import concourse.bass as bass
import concourse.mybir as mybir
from concourse.bass_utils import run_bass_kernel_spmd

F32 = mybir.dt.float32
BF16 = mybir.dt.bfloat16
I32 = mybir.dt.int32
AF = mybir.ActivationFunctionType
ALU = mybir.AluOpType
AX = mybir.AxisListType

B_, L, D, H, DH, HID = 8, 4096, 1024, 16, 64, 3072
NT = L // 128
NB = L // 512
NQ = L // 512
NC_ = L // 256
CSH = 20.0
EPS = 1e-5
ENGS = ("sync", "gpsimd", "tensor", "scalar", "vector")


class Sem:
    __slots__ = ("name", "val", "h")

    def __init__(self, name):
        self.name, self.val, self.h = name, 0, None


class Plan:
    def __init__(self):
        self.steps = {e: [] for e in ENGS}
        self.sems = []

    def sem(self, name):
        s = Sem(name)
        self.sems.append(s)
        return s

    def do(self, eng, fn, wait=(), inc=None):
        self.steps[eng].append((tuple(wait), fn, inc))
        if inc is not None:
            inc[0].val += inc[1]

    def at(self, *sems):
        return tuple((s, s.val) for s in sems)

    def emit(self, nc, block):
        for s in self.sems:
            s.h = nc._semstack.enter_context(nc.semaphore(s.name))
        for eng_name in ENGS:
            steps = self.steps[eng_name]
            if not steps:
                continue
            dec = getattr(block, eng_name)

            def body(eng, steps=steps):
                last = {}
                for waits, fn, inc in steps:
                    for s, v in waits:
                        if v > 0 and last.get(id(s), -1) < v:
                            eng.wait_ge(s.h, v)
                            last[id(s)] = v
                    ins = fn(eng)
                    if inc is not None:
                        ins.then_inc(inc[0].h, inc[1])

            dec(body)


def build_nc(debug=False):
    nc = bass.Bass("TRN2", target_bir_lowering=False)
    nc._semstack = ExitStack()

    x = nc.dram_tensor("x", [L, D], F32, kind="ExternalInput").ap()
    wqkvT = nc.dram_tensor("wqkvT", [D, 3 * D], F32, kind="ExternalInput").ap()
    wprojT = nc.dram_tensor("wprojT", [D, D], F32, kind="ExternalInput").ap()
    wgT = nc.dram_tensor("wgT", [D, HID], I32, kind="ExternalInput").ap()
    wuT = nc.dram_tensor("wuT", [D, HID], I32, kind="ExternalInput").ap()
    wdT = nc.dram_tensor("wdT", [HID, D], I32, kind="ExternalInput").ap()
    ident = nc.dram_tensor("ident", [128, 128], BF16, kind="ExternalInput").ap()
    maskbig = nc.dram_tensor("maskbig", [128, 128], BF16, kind="ExternalInput").ap()
    identf = nc.dram_tensor("identf", [65, 65], F32, kind="ExternalInput").ap()
    out = nc.dram_tensor("out", [L, D], F32, kind="ExternalOutput").ap()

    skind = "ExternalOutput" if debug else "Internal"
    qscr = nc.dram_tensor("qscr", [D, L], BF16, kind="Internal").ap()
    vscr = nc.dram_tensor("vscr", [NT, 128, 1040], BF16, kind="Internal").ap()
    wgbf = nc.dram_tensor("wgbf", [D, HID], BF16, kind="Internal").ap()
    wubf = nc.dram_tensor("wubf", [D, HID], BF16, kind="Internal").ap()
    wdbf = nc.dram_tensor("wdbf", [HID, D], BF16, kind="Internal").ap()
    x1scr = nc.dram_tensor("x1scr", [L, D], F32, kind=skind).ap()
    U16 = mybir.dt.uint16
    if debug:
        qdump = nc.dram_tensor("qdump", [D, L], U16, kind="ExternalOutput").ap()
        vdump = nc.dram_tensor("vdump", [NT, 128, 1040], U16, kind="ExternalOutput").ap()
        xndump = nc.dram_tensor("xndump", [L, D], U16, kind="ExternalOutput").ap()
        attdump = nc.dram_tensor("attdump", [NQ, 128, 8 * 512], U16, kind="ExternalOutput").ap()
        statdump = nc.dram_tensor("statdump", [6, 128, NT], F32, kind="ExternalOutput").ap()

    P = Plan()
    ld = P.sem("ld")
    sld = P.sem("sld")
    gld = P.sem("gld")
    gst = P.sem("gst")
    pe = P.sem("pe")
    ac = P.sem("ac")
    ve = P.sem("ve")

    with ExitStack() as glob:
        e = glob.enter_context
        sident = e(nc.sbuf_tensor("sident", [128, 128], BF16))
        smask = e(nc.sbuf_tensor("smask", [128, 128], BF16))
        sidentf = e(nc.sbuf_tensor("sidentf", [65, 65], F32))
        eps_sb = e(nc.sbuf_tensor("eps_sb", [128, 1], F32))
        bt_nm = e(nc.sbuf_tensor("bt_nm", [128, NT], F32))
        bt_rstd = e(nc.sbuf_tensor("bt_rstd", [128, NT], F32))
        ncsh_sb = e(nc.sbuf_tensor("ncsh_sb", [128, 1], F32))
        P.do("vector", lambda v: v.memset(eps_sb[:], EPS), inc=(ve, 1))
        P.do("vector", lambda v: v.memset(ncsh_sb[:], -CSH), inc=(ve, 1))

        P.do("sync", lambda s: s.dma_start(out=sident[:], in_=ident), inc=(ld, 16))
        P.do("sync", lambda s: s.dma_start(out=smask[:], in_=maskbig), inc=(ld, 16))
        P.do("sync", lambda s: s.dma_start(out=sidentf[:], in_=identf), inc=(ld, 16))
        consts_ld = P.at(ld)

        phab = glob.enter_context(ExitStack())
        KT = phab.enter_context(
            nc.sbuf_tensor("KT", [128, 8 * L], BF16))

        with ExitStack() as pha:
            ea = pha.enter_context
            wq_sb = ea(nc.sbuf_tensor("wq_sb", [128, 8 * 3 * D], BF16))
            xbuf = [ea(nc.sbuf_tensor(f"xbuf{i}", [128, D], F32)) for i in range(4)]
            xnbuf = [ea(nc.sbuf_tensor(f"xnbuf{i}", [128, D], BF16)) for i in range(4)]
            xnT = ea(nc.sbuf_tensor("xnT", [128, 8 * 512], BF16))
            qbuf = [ea(nc.sbuf_tensor(f"qbuf{i}", [128, 512], BF16)) for i in range(4)]
            vxbuf = [ea(nc.sbuf_tensor(f"vxbuf{i}", [128, 1040], BF16)) for i in range(2)]
            sqscr = ea(nc.sbuf_tensor("sqscr", [128, D], F32))
            st_sum = ea(nc.sbuf_tensor("st_sum", [128, NT], F32))
            st_ssq = ea(nc.sbuf_tensor("st_ssq", [128, NT], F32))
            st_nm = ea(nc.sbuf_tensor("st_nm", [128, NT], F32))
            st_m2 = ea(nc.sbuf_tensor("st_m2", [128, NT], F32))
            st_var = ea(nc.sbuf_tensor("st_var", [128, NT], F32))
            st_lnv = ea(nc.sbuf_tensor("st_lnv", [128, NT], F32))
            st_rstd = ea(nc.sbuf_tensor("st_rstd", [128, NT], F32))
            ps_tr = [ea(nc.psum_tensor(f"ps_tr{i}", [128, 1024], BF16)) for i in range(2)]
            ps_qkv = [ea(nc.psum_tensor(f"ps_qkv{i}", [128, 512], F32)) for i in range(4)]
            ps_v = ea(nc.psum_tensor("ps_v", [128, 1024], F32))

            marks = {}
            for j in range(8):
                P.do("gpsimd", lambda g, j=j: g.dma_start(
                    out=wq_sb[:, 3 * D * j:3 * D * (j + 1)],
                    in_=wqkvT[128 * j:128 * (j + 1), :]), inc=(gld, 16))
                marks[("wqld", j)] = P.at(gld)
            wq_ld = P.at(gld)
            for i in range(2):
                P.do("vector", lambda v, i=i: v.memset(vxbuf[i][:], 1.0), inc=(ve, 1))

            def plan_ln(i):
                bi = i % 4
                wt = list(marks.get(("apply", i - 4), ()))
                P.do("sync", lambda s, i=i, bi=bi: s.dma_start(
                    out=xbuf[bi][:], in_=x[128 * i:128 * (i + 1), :]),
                    wait=wt, inc=(ld, 16))
                xld = P.at(ld)
                cc = (i, i + 1)
                P.do("vector", lambda v, bi=bi, cc=cc: v.reduce_sum(
                    st_sum[:, cc[0]:cc[1]], xbuf[bi][:], AX.X), wait=xld, inc=(ve, 1))
                P.do("vector", lambda v, bi=bi, cc=cc: v.scalar_tensor_tensor(
                    sqscr[:], xbuf[bi][:], 1.0, xbuf[bi][:], ALU.mult, ALU.mult,
                    accum_out=st_ssq[:, cc[0]:cc[1]]), inc=(ve, 1))
                P.do("vector", lambda v, cc=cc: v.tensor_scalar_mul(
                    st_nm[:, cc[0]:cc[1]], st_sum[:, cc[0]:cc[1]], -1.0 / D),
                    wait=P.at(ve), inc=(ve, 1))
                P.do("vector", lambda v, cc=cc: v.tensor_mul(
                    st_m2[:, cc[0]:cc[1]], st_nm[:, cc[0]:cc[1]], st_nm[:, cc[0]:cc[1]]),
                    wait=P.at(ve), inc=(ve, 1))
                P.do("vector", lambda v, cc=cc: v.scalar_tensor_tensor(
                    st_var[:, cc[0]:cc[1]], st_ssq[:, cc[0]:cc[1]], 1.0 / D,
                    st_m2[:, cc[0]:cc[1]], ALU.mult, ALU.subtract),
                    wait=P.at(ve), inc=(ve, 1))
                stats = P.at(ve)
                P.do("scalar", lambda s, cc=cc: s.activation(
                    st_lnv[:, cc[0]:cc[1]], st_var[:, cc[0]:cc[1]], AF.Ln, bias=eps_sb[:]),
                    wait=stats, inc=(ac, 1))
                P.do("scalar", lambda s, cc=cc: s.activation(
                    st_rstd[:, cc[0]:cc[1]], st_lnv[:, cc[0]:cc[1]], AF.Exp, scale=-0.5),
                    wait=P.at(ac), inc=(ac, 1))
                marks[("rstd", i)] = P.at(ac)

            def plan_apply(i):
                bi = i % 4
                cc = (i, i + 1)
                P.do("vector", lambda v, bi=bi, cc=cc: v.tensor_scalar(
                    xnbuf[bi][:], xbuf[bi][:], st_nm[:, cc[0]:cc[1]],
                    st_rstd[:, cc[0]:cc[1]], ALU.add, ALU.mult),
                    wait=marks[("rstd", i)] + marks.get(("trans", i - 4), ()) +
                    (marks.get(("xndump", i - 4), ()) if debug else ()), inc=(ve, 1))
                marks[("apply", i)] = P.at(ve)
                if debug:
                    P.do("sync", lambda s, i=i, bi=bi: s.dma_start(
                        out=xndump[128 * i:128 * (i + 1), :].bitcast(U16),
                        in_=xnbuf[bi][:].bitcast(U16)),
                        wait=marks[("apply", i)], inc=(ld, 16))
                    marks[("xndump", i)] = P.at(ld)

            def plan_trans(i):
                t = i % 4
                bi = i % 4
                for j in range(8):
                    P.do("tensor", lambda t_, i=i, bi=bi, j=j: t_.matmul(
                        ps_tr[i % 2][:, 128 * j:128 * (j + 1)],
                        xnbuf[bi][:, 128 * j:128 * (j + 1)], sident[:],
                        is_transpose=True, start=True, stop=True,
                        skip_group_check=True),
                        wait=(marks[("apply", i)] + consts_ld +
                              marks.get(("trevac", i - 2), ())) if j == 0 else (),
                        inc=(pe, 1) if j == 7 else None)
                marks[("trans", i)] = P.at(pe)
                P.do("vector", lambda v, i=i, t=t: v.tensor_copy(
                    xnT[:].rearrange("p (j c) -> p j c", c=512)[:, :, 128 * t:128 * (t + 1)],
                    ps_tr[i % 2][:].rearrange("p (j c) -> p j c", c=128)),
                    wait=marks[("trans", i)], inc=(ve, 1))
                marks[("trevac", i)] = P.at(ve)

            for t_ in range(4):
                plan_ln(t_)
            for t_ in range(4):
                plan_apply(t_)
            for b in range(NB):
                i = 4 * b + 3
                for t_ in range(4):
                    plan_trans(4 * b + t_)
                if True:
                    xnT_ready = marks[("trevac", i)]
                    for w_ in range(16):
                        wg_ = 16 * b + w_
                        wbi = w_ % 4
                        for j in range(8):
                            P.do("tensor", lambda t_, wbi=wbi, j=j, w_=w_: t_.matmul(
                                ps_qkv[wbi][:],
                                wq_sb[:, 3 * D * j + 128 * w_:3 * D * j + 128 * (w_ + 1)],
                                xnT[:, 512 * j:512 * (j + 1)],
                                start=(j == 0), stop=(j == 7)),
                                wait=((xnT_ready +
                                      marks.get(("qkevac", wg_ - 4), ())) if j == 0
                                      else ()) + marks[("wqld", j)],
                                inc=(pe, 1) if j == 7 else None)
                        mm = P.at(pe)
                        if w_ < 8:
                            qc = 8 * b + w_
                            P.do("scalar", lambda s, wbi=wbi, qc=qc: s.copy(
                                qbuf[qc % 4][:], ps_qkv[wbi][:]),
                                wait=mm + marks.get(("qspill", qc - 4), ()), inc=(ac, 1))
                            marks[("qkevac", wg_)] = P.at(ac)
                            P.do("gpsimd", lambda g, w_=w_, b=b, qc=qc: g.dma_start(
                                out=qscr[128 * w_:128 * (w_ + 1), 512 * b:512 * (b + 1)],
                                in_=qbuf[qc % 4][:]),
                                wait=marks[("qkevac", wg_)], inc=(gst, 16))
                            marks[("qspill", qc)] = P.at(gst)
                            if w_ == 7 and b + 1 < NB:
                                for t_ in range(4):
                                    plan_ln(4 * (b + 1) + t_)
                        else:
                            kk = w_ - 8
                            P.do("scalar", lambda s, kk=kk, b=b, wbi=wbi: s.copy(
                                KT[:, L * kk + 512 * b:L * kk + 512 * (b + 1)],
                                ps_qkv[wbi][:]), wait=mm, inc=(ac, 1))
                            marks[("qkevac", wg_)] = P.at(ac)
                    if b + 1 < NB:
                        for t_ in range(4):
                            plan_apply(4 * (b + 1) + t_)
                    for tp in range(4):
                        tg = 4 * b + tp
                        vbi = tg % 2
                        for hf in range(2):
                            for j in range(8):
                                P.do("tensor", lambda t_, tp=tp, hf=hf, j=j: t_.matmul(
                                    ps_v[:, 512 * hf:512 * (hf + 1)],
                                    xnT[:, 512 * j + 128 * tp:512 * j + 128 * (tp + 1)],
                                    wq_sb[:, 3 * D * j + 2 * D + 512 * hf:
                                          3 * D * j + 2 * D + 512 * (hf + 1)],
                                    start=(j == 0), stop=(j == 7)),
                                    wait=marks.get(("vevac", tg - 1), ())
                                    if (j == 0 and hf == 0) else (),
                                    inc=(pe, 1) if (j == 7 and hf == 1) else None)
                        vm = P.at(pe)
                        P.do("vector", lambda v, vbi=vbi: v.tensor_copy(
                            vxbuf[vbi][:].rearrange("p (h c) -> p h c", c=65)[:, :, 0:64],
                            ps_v[:].rearrange("p (h c) -> p h c", c=64)),
                            wait=vm + marks.get(("vspill", tg - 2), ()), inc=(ve, 1))
                        marks[("vevac", tg)] = P.at(ve)
                        P.do("gpsimd", lambda g, b=b, tp=tp, vbi=vbi: g.dma_start(
                            out=vscr[4 * b + tp], in_=vxbuf[vbi][:]),
                            wait=marks[("vevac", tg)], inc=(gst, 16))
                        marks[("vspill", tg)] = P.at(gst)

            if debug:
                for n_, t_ in ((0, st_sum), (1, st_ssq), (2, st_nm), (3, st_rstd),
                               (4, st_m2), (5, st_var)):
                    P.do("sync", lambda s, n_=n_, t_=t_: s.dma_start(
                        out=statdump[n_], in_=t_[:]),
                        wait=P.at(ve, ac), inc=(ld, 16))
            barA = P.at(ld, gld, gst, pe, ac, ve)

        with ExitStack() as phb:
            eb = phb.enter_context
            AED = 7
            qt_sb = [eb(nc.sbuf_tensor(f"qt_sb{i}", [128, 8 * 512], BF16))
                     for i in range(2)]
            V_sb = eb(nc.sbuf_tensor("V_sb", [128, NT * 1040], BF16))
            ae = [eb(nc.sbuf_tensor(f"ae{i}", [128, 1024], BF16))
                  for i in range(AED)]
            attok = [eb(nc.sbuf_tensor(f"attok{i}", [128, 512], BF16))
                     for i in range(2)]
            attoutT = eb(nc.sbuf_tensor("attoutT", [128, 8 * 512], BF16))
            rec_sb = eb(nc.sbuf_tensor("rec_sb", [128, 8], F32))
            xtile = [eb(nc.sbuf_tensor(f"xtile{i}", [128, D], F32)) for i in range(2)]
            xp_sb = [eb(nc.sbuf_tensor(f"xp_sb{i}", [128, D], F32)) for i in range(2)]
            sqscr2 = eb(nc.sbuf_tensor("sqscr2", [128, D], BF16))
            bt_sum = eb(nc.sbuf_tensor("bt_sum", [128, NT], F32))
            bt_ssq = eb(nc.sbuf_tensor("bt_ssq", [128, NT], F32))
            bt_m2 = eb(nc.sbuf_tensor("bt_m2", [128, NT], F32))
            bt_var = eb(nc.sbuf_tensor("bt_var", [128, NT], F32))
            bt_lnv = eb(nc.sbuf_tensor("bt_lnv", [128, NT], F32))
            wp_sb = eb(nc.sbuf_tensor("wp_sb", [128, 8 * D], BF16))

            pool = eb(nc.psum_tensor("pbpool", [128, 4096], F32))
            poolb = pool.bitcast(BF16)
            psc = [pool[:, 0:1024], pool[:, 1024:2048]]
            trscratch = [poolb[:, 4096:4608], poolb[:, 6144:6656]]

            def av_region(p, hh, s):
                b = 4 + 2 * p + hh
                return pool[:, 512 * b + 65 * s:512 * b + 65 * (s + 1)]

            for j in range(8):
                P.do("gpsimd", lambda g, j=j: g.dma_start(
                    out=wp_sb[:, D * j:D * (j + 1)],
                    in_=wprojT[128 * j:128 * (j + 1), :]),
                    wait=barA if j == 0 else (), inc=(gld, 16))
            wp_ld = P.at(gld)
            precast = ()

            def plan_qload(q):
                P.do("sync", lambda s, q=q: s.dma_start(
                    out=qt_sb[q % 2][:].rearrange("p (j c) -> p j c", c=512),
                    in_=qscr[:, 512 * q:512 * (q + 1)].rearrange(
                        "(j p) c -> p j c", p=128)),
                    wait=(barA if q <= 1 else ()) +
                    marks.get(("scores_done", q - 2), ()), inc=(ld, 16))

            plan_qload(0)
            pending_stages = []
            steps_glob = []
            aidx = {}
            PSTEPS = 7

            def plan_sc_q(qt_, hp, kb, qv):
                d = kb - 4 * qt_
                off = 128 * d if d > 0 else 0
                wt = marks.get(("exp", qt_, hp, kb - 2), ())
                if kb <= 1:
                    if hp > 0:
                        wt = wt + marks.get(
                            ("exp", qt_, hp - 1, 4 * qt_ + 2 + kb), ())
                    else:
                        wt = wt + marks.get(
                            ("exp", qt_ - 1, 7, 4 * (qt_ - 1) + 2 + kb), ())
                if kb == 0:
                    wt = wt + qv + wp_ld
                    if hp == 0 and qt_ == 0:
                        wt = wt + barA
                for (hh, p0, tp) in ((0, 0, (0, 0)), (1, 64, (64, 0))):
                    P.do("tensor", lambda t_, kb=kb, hh=hh, p0=p0, tp=tp,
                         off=off, hp=hp, qt_=qt_: t_.matmul(
                             psc[kb % 2][:, 512 * hh + off:512 * (hh + 1)],
                             KT[p0:p0 + 64,
                                L * hp + 128 * kb:L * hp + 128 * (kb + 1)],
                             qt_sb[qt_ % 2][p0:p0 + 64,
                                            512 * hp + off:512 * (hp + 1)],
                             start=True, stop=True, tile_position=tp,
                             skip_group_check=True),
                         wait=wt if hh == 0 else (),
                         inc=(pe, 1) if hh == 1 else None)
                marks[("sc", qt_, hp, kb)] = P.at(pe)

            def emit_step(qt_, hp, kb, qv):
                if ("exp", qt_, hp, kb) in marks:
                    return
                if ("sc", qt_, hp, kb) not in marks:
                    plan_sc_q(qt_, hp, kb, qv)
                d = kb - 4 * qt_
                step = len(steps_glob)
                steps_glob.append((qt_, hp, kb))
                ai = step % AED
                aidx[(qt_, hp, kb)] = ai
                aefree = (marks.get(("av",) + steps_glob[step - AED], ())
                          if step >= AED else ())
                wex = marks[("sc", qt_, hp, kb)] + aefree
                if d <= 0:
                    P.do("scalar", lambda s, kb=kb, ai=ai: s.activation(
                        ae[ai][:], psc[kb % 2][:], AF.Exp,
                        bias=ncsh_sb[:], scale=0.125),
                        wait=wex, inc=(ac, 1))
                else:
                    for hh in range(2):
                        P.do("scalar", lambda s, kb=kb, ai=ai, hh=hh,
                             d=d: s.activation(
                                 ae[ai][:, 512 * hh + 128 * d:
                                        512 * (hh + 1)],
                                 psc[kb % 2][:, 512 * hh + 128 * d:
                                             512 * (hh + 1)],
                                 AF.Exp, bias=ncsh_sb[:], scale=0.125),
                             wait=wex if hh == 0 else (), inc=(ac, 1))
                marks[("exp", qt_, hp, kb)] = P.at(ac)
                if d >= 0:
                    for hh in range(2):
                        P.do("vector", lambda v, ai=ai, hh=hh, d=d:
                             v.tensor_mul(
                                 ae[ai][:, 512 * hh + 128 * d:
                                        512 * hh + 128 * (d + 1)],
                                 ae[ai][:, 512 * hh + 128 * d:
                                        512 * hh + 128 * (d + 1)],
                                 smask[:, 0:128]),
                             wait=(marks[("exp", qt_, hp, kb)] + consts_ld)
                             if hh == 0 else (),
                             inc=(ve, 1))
                    marks[("mask", qt_, hp, kb)] = P.at(ve)

            for qt in range(NQ):
                if qt == 3:
                    for j in range(8):
                        P.do("gpsimd", lambda g, j=j: g.dma_start(
                            out=wgbf[128 * j:128 * (j + 1), :],
                            in_=wgT[128 * j:128 * (j + 1), :]), inc=(gst, 16))
                    for j in range(8):
                        P.do("gpsimd", lambda g, j=j: g.dma_start(
                            out=wubf[128 * j:128 * (j + 1), :],
                            in_=wuT[128 * j:128 * (j + 1), :]), inc=(gst, 16))
                    for k in range(24):
                        P.do("gpsimd", lambda g, k=k: g.dma_start(
                            out=wdbf[128 * k:128 * (k + 1), :],
                            in_=wdT[128 * k:128 * (k + 1), :]), inc=(gst, 16))
                    precast = P.at(gst)
                if qt + 1 < NQ:
                    plan_qload(qt + 1)
                for kb in range(4 * qt, 4 * qt + 4):
                    P.do("sync", lambda s, kb=kb: s.dma_start(
                        out=V_sb[:, 1040 * kb:1040 * (kb + 1)], in_=vscr[kb]),
                        wait=barA if qt == 0 and kb == 0 else (), inc=(ld, 16))
                qv_ld = P.at(ld)

                nkb = 4 * qt + 4
                for hp in range(8):
                    hpg = 8 * qt + hp
                    p = hp % 2
                    for kb in range(nkb):
                        emit_step(qt, hp, kb, qv_ld)
                        if (kb + 1 < nkb and
                                ("sc", qt, hp, kb + 1) not in marks):
                            plan_sc_q(qt, hp, kb + 1, qv_ld)
                        d = kb - 4 * qt
                        ai = aidx[(qt, hp, kb)]
                        wav = (marks[("exp", qt, hp, kb)] +
                               marks.get(("mask", qt, hp, kb), ()))
                        if kb == 0:
                            wav = wav + marks.get(("cpy", hpg - 2), ())
                            if hp == 0:
                                wav = wav + marks.get(("xpadd", qt - 1, 0), ())
                            elif hp == 1:
                                wav = wav + marks.get(("xpadd", qt - 1, 3), ())
                        mml = [(s, hh) for s in range(4) if d <= s
                               for hh in range(2)]
                        for i_, (s, hh) in enumerate(mml):
                            P.do("tensor", lambda t_, kb=kb, s=s, hh=hh, hp=hp,
                                 p=p, qt=qt, ai=ai: t_.matmul(
                                     av_region(p, hh, s),
                                     ae[ai][:, 512 * hh + 128 * s:
                                            512 * hh + 128 * (s + 1)],
                                     V_sb[:, 1040 * kb + 65 * (2 * hp + hh):
                                          1040 * kb + 65 * (2 * hp + hh) + 65],
                                     start=(kb == 0 and s == 0),
                                     stop=(kb == 4 * qt + s),
                                     skip_group_check=True),
                                 wait=wav if i_ == 0 else (),
                                 inc=(pe, 1) if i_ == len(mml) - 1 else None)
                        marks[("av", qt, hp, kb)] = P.at(pe)
                        if pending_stages:
                            pending_stages.pop(0)()
                    if hp == 7:
                        marks[("scores_done", qt)] = P.at(pe)

                    def make_evac(qt=qt, hp=hp, hpg=hpg, p=p,
                                  avm=marks[("av", qt, hp, nkb - 1)]):
                        def s_recip():
                            for hh in range(2):
                                b = 4 + 2 * p + hh
                                P.do("vector", lambda v, b=b, hh=hh: v.reciprocal(
                                    rec_sb[:, 4 * hh:4 * hh + 4],
                                    pool[:, 512 * b:512 * b + 260].rearrange(
                                        "p (s w) -> p s w", w=65)[:, :, 64]),
                                    wait=avm if hh == 0 else (), inc=(ve, 1))
                            marks[("recip", hpg)] = P.at(ve)

                        def s_norm():
                            wt = (marks[("recip", hpg)] +
                                  marks.get(("fwdT", hpg - 2), ()))
                            i_ = 0
                            for s in range(4):
                                for hh in range(2):
                                    P.do("vector", lambda v, s=s, hh=hh, p=p:
                                         v.tensor_scalar_mul(
                                             attok[p][:, 128 * s + 64 * hh:
                                                      128 * s + 64 * (hh + 1)],
                                             av_region(p, hh, s)[:, 0:64],
                                             rec_sb[:, 4 * hh + s:4 * hh + s + 1]),
                                         wait=wt if i_ == 0 else (), inc=(ve, 1))
                                    i_ += 1
                            marks[("norm", hpg)] = P.at(ve)

                        def s_fwdT():
                            wt = (marks[("norm", hpg)] +
                                  marks.get(("cpy", hpg - 2), ()))
                            for s in range(4):
                                P.do("tensor", lambda t_, s=s, p=p: t_.matmul(
                                    trscratch[p][:, 128 * s:128 * (s + 1)],
                                    attok[p][:, 128 * s:128 * (s + 1)], sident[:],
                                    is_transpose=True, start=True, stop=True,
                                    skip_group_check=True),
                                    wait=wt if s == 0 else (),
                                    inc=(pe, 1) if s == 3 else None)
                            marks[("fwdT", hpg)] = P.at(pe)

                        def s_cpy():
                            wt = marks[("fwdT", hpg)] + \
                                (marks.get(("attdump", qt - 1), ())
                                 if debug else ())
                            P.do("vector", lambda v, hp=hp, p=p: v.tensor_copy(
                                attoutT[:, 512 * hp:512 * (hp + 1)],
                                trscratch[p][:]),
                                wait=wt, inc=(ve, 1))
                            marks[("cpy", hpg)] = P.at(ve)

                        return [s_recip, s_norm, s_fwdT, s_cpy]

                    while pending_stages:
                        pending_stages.pop(0)()
                    pending_stages.extend(make_evac())
                if qt + 1 < NQ:
                    qvp = P.at(ld)
                    for kb in range(PSTEPS):
                        emit_step(qt + 1, 0, kb, qvp)
                while pending_stages:
                    pending_stages.pop(0)()
                attout_rdy = marks[("cpy", 8 * qt + 7)]
                if debug:
                    P.do("sync", lambda s, qt=qt: s.dma_start(
                        out=attdump[qt].bitcast(U16), in_=attoutT[:].bitcast(U16)),
                        wait=attout_rdy, inc=(ld, 16))
                    marks[("attdump", qt)] = P.at(ld)

                def plan_xload(tp, qt=qt):
                    wt = (marks.get(("xpadd", qt, tp - 2), ()) if tp >= 2
                          else marks.get(("xpadd", qt - 1, tp + 2), ()))
                    P.do("sync", lambda s, qt=qt, tp=tp: s.dma_start(
                        out=xtile[tp % 2][:],
                        in_=x[512 * qt + 128 * tp:512 * qt + 128 * (tp + 1), :]),
                        wait=wt, inc=(ld, 16))
                    marks[("xld", qt, tp)] = P.at(ld)

                for tp in range(2):
                    plan_xload(tp)

                for s_ in range(4):
                    pcol = 2048 if s_ == 0 else 3072
                    for hf in range(2):
                        for j in range(8):
                            wt_ = ()
                            if j == 0 and hf == 0:
                                wt_ = (attout_rdy if s_ < 2 else
                                       marks[("xpadd", qt, s_ - 1)])
                            P.do("tensor", lambda t_, s_=s_, hf=hf, j=j,
                                 pcol=pcol: t_.matmul(
                                     pool[:, pcol + 512 * hf:
                                          pcol + 512 * (hf + 1)],
                                     attoutT[:, 512 * j + 128 * s_:
                                             512 * j + 128 * (s_ + 1)],
                                     wp_sb[:, D * j + 512 * hf:
                                           D * j + 512 * (hf + 1)],
                                     start=(j == 0), stop=(j == 7),
                                     skip_group_check=True),
                                 wait=wt_,
                                 inc=(pe, 1) if (j == 7 and hf == 1) else None)
                    marks[("projm", qt, s_)] = P.at(pe)
                    P.do("vector", lambda v, s_=s_, pcol=pcol: v.tensor_add(
                        xp_sb[s_ % 2][:], xtile[s_ % 2][:],
                        pool[:, pcol:pcol + 1024]),
                        wait=marks[("projm", qt, s_)] + marks[("xld", qt, s_)] +
                        (marks.get(("xpspill", qt, s_ - 2), ()) if s_ >= 2
                         else marks.get(("xpspill", qt - 1, s_ + 2), ())),
                        inc=(ve, 1))
                    marks[("xpadd", qt, s_)] = P.at(ve)
                    if s_ + 2 < 4:
                        plan_xload(s_ + 2)
                    P.do("gpsimd", lambda g, qt=qt, s_=s_: g.dma_start(
                        out=x1scr[512 * qt + 128 * s_:512 * qt + 128 * (s_ + 1), :],
                        in_=xp_sb[s_ % 2][:]),
                        wait=marks[("xpadd", qt, s_)], inc=(gst, 16))
                    marks[("xpspill", qt, s_)] = P.at(gst)
                    cc = (4 * qt + s_, 4 * qt + s_ + 1)
                    P.do("vector", lambda v, cc=cc, s_=s_: v.reduce_sum(
                        bt_sum[:, cc[0]:cc[1]], xp_sb[s_ % 2][:], AX.X),
                        inc=(ve, 1))
                    P.do("vector", lambda v, cc=cc, s_=s_: v.scalar_tensor_tensor(
                        sqscr2[:], xp_sb[s_ % 2][:], 1.0, xp_sb[s_ % 2][:],
                        ALU.mult, ALU.mult,
                        accum_out=bt_ssq[:, cc[0]:cc[1]]), inc=(ve, 1))
                    P.do("vector", lambda v, cc=cc: v.tensor_scalar_mul(
                        bt_nm[:, cc[0]:cc[1]], bt_sum[:, cc[0]:cc[1]], -1.0 / D),
                        wait=P.at(ve), inc=(ve, 1))
                    P.do("vector", lambda v, cc=cc: v.tensor_mul(
                        bt_m2[:, cc[0]:cc[1]], bt_nm[:, cc[0]:cc[1]],
                        bt_nm[:, cc[0]:cc[1]]), wait=P.at(ve), inc=(ve, 1))
                    P.do("vector", lambda v, cc=cc: v.scalar_tensor_tensor(
                        bt_var[:, cc[0]:cc[1]], bt_ssq[:, cc[0]:cc[1]], 1.0 / D,
                        bt_m2[:, cc[0]:cc[1]], ALU.mult, ALU.subtract),
                        wait=P.at(ve), inc=(ve, 1))
                    marks[("bstat", qt, s_)] = P.at(ve)

                def make_ln2fin(qt=qt):
                    outl = []
                    for s_ in range(4):
                        def fin(s_=s_, qt=qt):
                            cc = (4 * qt + s_, 4 * qt + s_ + 1)
                            P.do("scalar", lambda s, cc=cc: s.activation(
                                bt_lnv[:, cc[0]:cc[1]], bt_var[:, cc[0]:cc[1]],
                                AF.Ln, bias=eps_sb[:]),
                                wait=marks[("bstat", qt, s_)], inc=(ac, 1))
                            P.do("scalar", lambda s, cc=cc: s.activation(
                                bt_rstd[:, cc[0]:cc[1]], bt_lnv[:, cc[0]:cc[1]],
                                AF.Exp, scale=-0.5), wait=P.at(ac), inc=(ac, 1))
                            marks[("bt_rstd", qt, s_)] = P.at(ac)
                        outl.append(fin)
                    return outl

                pending_stages.extend([lambda: None] * min(12, 4 * (qt + 1)))
                pending_stages.extend(make_ln2fin())
                if qt == NQ - 1:
                    while pending_stages:
                        pending_stages.pop(0)()

            barB = P.at(ld, gld, gst, pe, ac, ve)

        phab.close()

        with ExitStack() as phc:
            ec = phc.enter_context
            wg_sb = ec(nc.sbuf_tensor("wg_sb", [128, 8 * HID], BF16))
            xpblk = [ec(nc.sbuf_tensor(f"xpblk{i}", [128, 2 * D], F32))
                     for i in range(3)]
            wu_sb = ec(nc.sbuf_tensor("wu_sb", [128, 8 * HID], BF16))
            wd_sb = ec(nc.sbuf_tensor("wd_sb", [128, 24 * D], BF16))
            xn2blk = [ec(nc.sbuf_tensor(f"xn2blk{i}", [128, 2 * D], BF16)) for i in range(2)]
            xn2T = ec(nc.sbuf_tensor("xn2T", [128, 8 * 256], BF16))
            hT = ec(nc.sbuf_tensor("hT", [128, 24 * 256], BF16))
            silu_sb = [ec(nc.sbuf_tensor(f"silu{i}", [128, 256], F32)) for i in range(2)]
            out_sb = [ec(nc.sbuf_tensor(f"out_sb{i}", [128, D], F32)) for i in range(2)]
            ps_tr2 = [ec(nc.psum_tensor(f"ps_tr2{i}", [128, 1024], BF16)) for i in range(2)]
            ps_gu = [ec(nc.psum_tensor(f"ps_gu{i}", [128, 512], F32)) for i in range(2)]
            ps_d = [ec(nc.psum_tensor(f"ps_d{i}", [128, 1024], F32)) for i in range(2)]

            wrdy = marks[("scores_done", NQ - 1)] + precast
            for j in range(8):
                P.do("scalar", lambda s, j=j: s.dma_start(
                    out=wg_sb[:, HID * j:HID * (j + 1)],
                    in_=wgbf[128 * j:128 * (j + 1), :]),
                    wait=wrdy if j == 0 else (), inc=(sld, 16))
                marks[("wg_ld", j)] = P.at(sld)
            for k in range(24):
                P.do("gpsimd", lambda g, k=k: g.dma_start(
                    out=wd_sb[:, D * k:D * (k + 1)],
                    in_=wdbf[128 * k:128 * (k + 1), :]),
                    wait=(barB + precast) if k == 0 else (), inc=(gld, 16))
                marks[("wd_ld", k)] = P.at(gld)

            def plan_cdma(c):
                P.do("sync", lambda s, c=c: s.dma_start(
                    out=xpblk[c % 3][:].rearrange("p (t d) -> p t d", d=D),
                    in_=x1scr[256 * c:256 * (c + 1), :].rearrange(
                        "(t p) d -> p t d", p=128)),
                    wait=wrdy + marks.get(("outev", c - 3, 1), ()), inc=(ld, 16))
                marks[("cld", c)] = P.at(ld)

            def plan_xn2(c):
                for t_ in range(2):
                    P.do("vector", lambda v, c=c, t_=t_: v.tensor_scalar(
                        xn2blk[c % 2][:, D * t_:D * (t_ + 1)],
                        xpblk[c % 3][:, D * t_:D * (t_ + 1)],
                        bt_nm[:, 2 * c + t_:2 * c + t_ + 1],
                        bt_rstd[:, 2 * c + t_:2 * c + t_ + 1],
                        ALU.add, ALU.mult),
                        wait=(marks[("cld", c)] + barB +
                              marks.get(("ctr", c - 2), ()))
                        if t_ == 0 else (), inc=(ve, 1))
                marks[("xn2c", c)] = P.at(ve)

            plan_cdma(0)
            for j in range(2):
                P.do("sync", lambda s, j=j: s.dma_start(
                    out=wu_sb[:, HID * j:HID * (j + 1)],
                    in_=wubf[128 * j:128 * (j + 1), :]),
                    wait=wrdy if j == 0 else (), inc=(ld, 16))
                marks[("wu_ld", j)] = P.at(ld)
            plan_cdma(1)
            for j in range(2, 8):
                P.do("sync", lambda s, j=j: s.dma_start(
                    out=wu_sb[:, HID * j:HID * (j + 1)],
                    in_=wubf[128 * j:128 * (j + 1), :]), inc=(ld, 16))
                marks[("wu_ld", j)] = P.at(ld)
            plan_xn2(0)
            plan_xn2(1)

            for c in range(NC_):
                cbi = c % 2
                cxi = c % 3
                if c + 2 < NC_:
                    plan_cdma(c + 2)
                cld = marks[("cld", c)]
                for tpp in range(2):
                    for j in range(8):
                        P.do("tensor", lambda t_, tpp=tpp, j=j, cbi=cbi: t_.matmul(
                            ps_tr2[tpp][:, 128 * j:128 * (j + 1)],
                            xn2blk[cbi][:, D * tpp + 128 * j:D * tpp + 128 * (j + 1)],
                            sident[:],
                            is_transpose=True, start=True, stop=True,
                            skip_group_check=True),
                            wait=(marks[("xn2c", c)] +
                                  marks.get(("gu_done", c - 1), ()) +
                                  marks.get(("ctrev", c - 1), ()) +
                                  (barB if c == 0 else ())) if (j == 0 and tpp == 0)
                            else (),
                            inc=(pe, 1) if j == 7 else None)
                marks[("ctr", c)] = P.at(pe)
                for tpp in range(2):
                    P.do("vector", lambda v, tpp=tpp: v.tensor_copy(
                        xn2T[:].rearrange("p (j d) -> p j d", d=256)[:, :, 128 * tpp:
                                                                    128 * (tpp + 1)],
                        ps_tr2[tpp][:].rearrange("p (j d) -> p j d", d=128)),
                        wait=marks[("ctr", c)], inc=(ve, 1))
                marks[("ctrev", c)] = P.at(ve)
                for w_ in range(24):
                    wbi = w_ % 2
                    for (col0, wsb) in ((0, wg_sb), (256, wu_sb)):
                        for j in range(8):
                            wt_ = ()
                            if j == 0 and col0 == 0:
                                wt_ = (marks[("ctrev", c)] +
                                       marks.get(("hev", c, w_ - 2), ()))
                            if c == 0:
                                wt_ = wt_ + marks[("wg_ld" if col0 == 0
                                                   else "wu_ld", j)]
                            P.do("tensor", lambda t_, wbi=wbi, col0=col0, wsb=wsb,
                                 j=j, w_=w_: t_.matmul(
                                     ps_gu[wbi][:, col0:col0 + 256],
                                     wsb[:, HID * j + 128 * w_:HID * j + 128 * (w_ + 1)],
                                     xn2T[:, 256 * j:256 * (j + 1)],
                                     start=(j == 0), stop=(j == 7), skip_group_check=True),
                                 wait=wt_,
                                 inc=(pe, 1) if (j == 7 and col0 == 256) else None)
                    gum = P.at(pe)
                    P.do("scalar", lambda s, wbi=wbi: s.activation(
                        silu_sb[wbi][:], ps_gu[wbi][:, 0:256], AF.Silu, scale=1.0 / 127.0),
                        wait=gum + marks.get(("hev", c, w_ - 2), ()), inc=(ac, 1))
                    sm = P.at(ac)
                    P.do("vector", lambda v, wbi=wbi, w_=w_: v.scalar_tensor_tensor(
                        hT[:, 256 * w_:256 * (w_ + 1)], ps_gu[wbi][:, 256:512],
                        1.0 / 16129.0, silu_sb[wbi][:], ALU.mult, ALU.mult),
                        wait=sm, inc=(ve, 1))
                    marks[("hev", c, w_)] = P.at(ve)
                marks[("gu_done", c)] = P.at(pe)
                h_rdy = marks[("hev", c, 23)]
                for tpp in range(2):
                    dbi = tpp % 2
                    for hf in range(2):
                        for k in range(24):
                            wt_ = ()
                            if k == 0 and hf == 0:
                                wt_ = h_rdy + marks.get(("outev", c - 1, tpp), ())
                            if c == 0 and tpp == 0 and hf == 0:
                                wt_ = wt_ + marks[("wd_ld", k)]
                            P.do("tensor", lambda t_, dbi=dbi, hf=hf, k=k, tpp=tpp:
                                 t_.matmul(
                                     ps_d[dbi][:, 512 * hf:512 * (hf + 1)],
                                     hT[:, 256 * k + 128 * tpp:256 * k + 128 * (tpp + 1)],
                                     wd_sb[:, D * k + 512 * hf:D * k + 512 * (hf + 1)],
                                     start=(k == 0), stop=(k == 23),
                                     skip_group_check=True),
                                 wait=wt_,
                                 inc=(pe, 1) if (k == 23 and hf == 1) else None)
                    dm = P.at(pe)
                    P.do("vector", lambda v, dbi=dbi, tpp=tpp, cxi=cxi: v.tensor_add(
                        out_sb[dbi][:], xpblk[cxi][:, D * tpp:D * (tpp + 1)],
                        ps_d[dbi][:]),
                        wait=dm + cld + marks.get(("outsp", c - 1, tpp), ()), inc=(ve, 1))
                    marks[("outev", c, tpp)] = P.at(ve)
                    P.do("gpsimd", lambda g, c=c, tpp=tpp, dbi=dbi: g.dma_start(
                        out=out[256 * c + 128 * tpp:256 * c + 128 * (tpp + 1), :],
                        in_=out_sb[dbi][:]),
                        wait=marks[("outev", c, tpp)], inc=(gst, 16))
                    marks[("outsp", c, tpp)] = P.at(gst)
                if c + 2 < NC_:
                    plan_xn2(c + 2)

            if debug:
                P.do("sync", lambda s: s.dma_start(out=qdump, in_=qscr.bitcast(U16)),
                     wait=P.at(gst, ve, ac, pe), inc=(ld, 16))
                P.do("sync", lambda s: s.dma_start(out=vdump, in_=vscr.bitcast(U16)),
                     inc=(ld, 16))
                P.do("sync", lambda s: s.nop(), wait=P.at(ld))
            P.do("gpsimd", lambda g: g.nop(), wait=P.at(gst))

        block = glob.enter_context(nc.Block())
        P.emit(nc, block)

    nc._semstack.close()
    return nc


_NC_CACHE = {}


def _get_nc(debug=False):
    if debug not in _NC_CACHE:
        _NC_CACHE[debug] = build_nc(debug)
    return _NC_CACHE[debug]


def _consts():
    ident = np.eye(128, dtype=ml_dtypes.bfloat16)
    mb = (np.arange(128)[None, :] >= np.arange(128)[:, None])
    maskbig = mb.astype(ml_dtypes.bfloat16)
    identf = np.eye(65, dtype=np.float32)
    return ident, maskbig, identf


def kernel(L0_attn_x, L0_attn_w_qkv, L0_attn_w_proj, L0_ffn_w_gate,
           L0_ffn_w_up, L0_ffn_w_down, _debug=False):
    x = np.ascontiguousarray(np.asarray(L0_attn_x, dtype=np.float32))
    wqkvT = np.ascontiguousarray(np.asarray(L0_attn_w_qkv, dtype=np.float32).T)
    wprojT = np.ascontiguousarray(np.asarray(L0_attn_w_proj, dtype=np.float32).T)
    wgT = np.ascontiguousarray(np.asarray(L0_ffn_w_gate, dtype=np.int32).T)
    wuT = np.ascontiguousarray(np.asarray(L0_ffn_w_up, dtype=np.int32).T)
    wdT = np.ascontiguousarray(np.asarray(L0_ffn_w_down, dtype=np.int32).T)
    ident, maskbig, identf = _consts()

    nc = _get_nc(_debug)
    common = {"wqkvT": wqkvT, "wprojT": wprojT, "wgT": wgT, "wuT": wuT,
              "wdT": wdT, "ident": ident, "maskbig": maskbig, "identf": identf}
    in_maps = [dict(common, x=np.ascontiguousarray(x[c])) for c in range(B_)]
    res = run_bass_kernel_spmd(nc, in_maps, core_ids=list(range(B_)))
    outs = np.stack([np.asarray(res.results[c]["out"]) for c in range(B_)], axis=0)
    if _debug:
        return outs, res.results
    return outs
